# version 02
# baseline (speedup 1.0000x reference)
import contextlib
import numpy as np
import ml_dtypes
import concourse.bass as bass
import concourse.mybir as mybir
from concourse.bass_utils import run_bass_kernel_spmd

F32 = mybir.dt.float32
BF16 = mybir.dt.bfloat16
ALU = mybir.AluOpType
AF = mybir.ActivationFunctionType
AX = mybir.AxisListType.X
BF = ml_dtypes.bfloat16

S = 2048
D = 4096
QT = 512
NQ = S // QT
NIC = D // 128
EPS = 1e-5
ENGS = ("tensor", "vector", "scalar", "gpsimd", "sync")

PERM = np.concatenate([np.arange(0, 32), np.arange(64, 96),
                       np.arange(32, 64), np.arange(96, 128)])


def tiled_w(w):
    O, I = w.shape
    co = min(128, O)
    noc, nic = O // co, I // 128
    return np.ascontiguousarray(
        w.reshape(noc, co, nic, 128).transpose(0, 3, 2, 1).reshape(noc, 128, nic * co))


def rhs_tiled(w):
    return np.ascontiguousarray(w.T.reshape(w.shape[1] // 128, 128, w.shape[0]))


def bias_tiled(b):
    return np.ascontiguousarray(b.reshape(-1, 128).T)


def rope_tables():
    inv_freq = 1.0 / 10000.0 ** (np.arange(0, 128, 2, dtype=np.float32) / 128)
    pos = np.arange(S, dtype=np.float32)
    freqs = np.outer(pos, inv_freq)
    emb = np.concatenate([freqs, freqs], axis=-1)
    cos, sin = np.cos(emb), np.sin(emb)
    qs = 1.0 / np.sqrt(128.0)
    cos_q = cos[:, PERM].T * qs
    sin_q = sin[:, PERM].T * qs
    sgn = np.ones(128, dtype=np.float32)
    sgn[0:32] = -1.0
    sgn[64:96] = -1.0
    sin_q = sin_q * sgn[:, None]
    rl = np.concatenate([np.arange(0, 32), np.arange(64, 96)])
    cos_k = np.tile(cos[:, rl].T, (2, 1))
    sin_k = np.tile(sin[:, rl].T, (2, 1))
    sgn64 = np.ones(64, dtype=np.float32)
    sgn64[0:32] = -1.0
    sin_k = sin_k * np.tile(sgn64, 2)[:, None]
    return (cos_q.astype(np.float32), sin_q.astype(np.float32),
            cos_k.astype(np.float32), sin_k.astype(np.float32))


def make_consts():
    c = {}
    c["c_cosq"], c["c_sinq"], c["c_cosk"], c["c_sink"] = rope_tables()
    m = np.zeros((128, 4 * 512), dtype=BF)
    for j in range(4):
        kt = j * 128 + np.arange(128)[:, None]
        m[:, j * 512:(j + 1) * 512] = (kt <= np.arange(512)[None, :]).astype(BF)
    c["c_maskt"] = m
    c["c_ident"] = np.eye(128, dtype=BF)
    c["c_ones"] = np.ones((128, 1), dtype=BF)
    c["c_onesr"] = np.ones((1, 128), dtype=BF)
    e8 = np.zeros((8, 128), dtype=BF)
    for e in range(8):
        e8[e, e * 16:(e + 1) * 16] = 1
    c["c_eexp8"] = e8
    es = np.zeros((16, 8 * 128), dtype=BF)
    for cc in range(8):
        es[2 * cc, cc * 128:cc * 128 + 64] = 1
        es[2 * cc + 1, cc * 128 + 64:cc * 128 + 128] = 1
    c["c_esel"] = es
    return c


def prep_weights(inp):
    w = {}
    wq = np.asarray(inp["L0_attn_Wq"], dtype=np.float32)
    wq_p = wq.reshape(32, 128, D)[:, PERM, :].reshape(D, D)
    w["w_l0q"] = tiled_w(wq_p)
    w["w_l0kr"] = tiled_w(np.asarray(inp["L0_attn_Wkr"], dtype=np.float32))
    w["w_l0dk"] = tiled_w(np.asarray(inp["L0_attn_Wdk"], dtype=np.float32))
    w["w_l0upk"] = tiled_w(np.asarray(inp["L0_attn_Wupk"], dtype=np.float32))
    w["w_l0upv"] = rhs_tiled(np.asarray(inp["L0_attn_Wupv"], dtype=np.float32))
    w["w_l0o"] = tiled_w(np.asarray(inp["L0_attn_Wo"], dtype=np.float32))
    bcomb = np.asarray(inp["L0_ffn_B_w"], dtype=np.float32).transpose(1, 0, 2).reshape(D, 128)
    w["w_l0base"] = np.ascontiguousarray(np.concatenate(
        [tiled_w(np.asarray(inp["L0_ffn_W_base"], dtype=np.float32)),
         tiled_w(bcomb)], axis=2))
    w["b_l0base"] = bias_tiled(np.asarray(inp["L0_ffn_b_base"], dtype=np.float32))
    gt = np.concatenate([np.asarray(inp["L0_ffn_W_gate"], dtype=np.float32),
                         np.asarray(inp["L0_ffn_W_thr"], dtype=np.float32)[None, :]], 0)
    w["w_l0gate"] = rhs_tiled(gt)
    w["b_l0thr"] = np.asarray(inp["L0_ffn_b_thr"], dtype=np.float32).reshape(1, 1)
    w["w_l0A"] = tiled_w(np.asarray(inp["L0_ffn_A_w"], dtype=np.float32).reshape(128, D))
    w["w_l1q"] = tiled_w(np.asarray(inp["L1_attn_wq_w"], dtype=np.float32))
    w["b_l1q"] = bias_tiled(np.asarray(inp["L1_attn_wq_b"], dtype=np.float32))
    w["w_l1k"] = tiled_w(np.asarray(inp["L1_attn_wk_w"], dtype=np.float32))
    w["b_l1k"] = np.asarray(inp["L1_attn_wk_b"], dtype=np.float32).reshape(128, 1)
    w["w_l1v"] = tiled_w(np.asarray(inp["L1_attn_wv_w"], dtype=np.float32))
    w["b_l1v"] = np.asarray(inp["L1_attn_wv_b"], dtype=np.float32).reshape(128, 1)
    w["w_l1o"] = tiled_w(np.asarray(inp["L1_attn_wo_w"], dtype=np.float32))
    w["b_l1o"] = bias_tiled(np.asarray(inp["L1_attn_wo_b"], dtype=np.float32))
    bc1 = np.asarray(inp["L1_ffn_B"], dtype=np.float32).transpose(1, 0, 2).reshape(D, 1024)
    w["w_l1base"] = np.ascontiguousarray(np.concatenate(
        [tiled_w(np.asarray(inp["L1_ffn_base_w"], dtype=np.float32)),
         tiled_w(bc1)], axis=2))
    w["b_l1base"] = bias_tiled(np.asarray(inp["L1_ffn_base_b"], dtype=np.float32))
    w["w_l1A"] = tiled_w(np.asarray(inp["L1_ffn_A"], dtype=np.float32))
    w["w_l1r"] = rhs_tiled(np.asarray(inp["L1_ffn_router_w"], dtype=np.float32))
    return w


class Op:
    __slots__ = ("eng", "fn", "waits", "inc", "idx", "need_inc", "val", "tag",
                 "sem_key")

    def __init__(self, eng, fn, waits, inc, idx, tag, sem_key):
        self.eng, self.fn, self.waits, self.inc = eng, fn, waits, inc
        self.idx, self.need_inc, self.val, self.tag = idx, False, None, tag
        self.sem_key = sem_key


class Buf:
    __slots__ = ("name", "last_w", "readers")

    def __init__(self, name):
        self.name, self.last_w, self.readers = name, None, []


class Prog:
    def __init__(self, nc):
        self.nc = nc
        self.ops = {e: [] for e in ENGS}
        self.all_ops = []
        self.sem_keys = list(ENGS)

    def op(self, eng, fn, reads=(), writes=(), dma=False, tag=""):
        waits = set()
        for b in reads:
            if b.last_w is not None:
                waits.add(b.last_w)
        for b in writes:
            if b.last_w is not None:
                waits.add(b.last_w)
            waits.update(b.readers)
        if dma:
            assert len(writes) == 1, f"dma op {tag} must write exactly one Buf"
            sem_key = "d_" + writes[0].name + ("_g" if eng == "gpsimd" else "")
            if sem_key not in self.sem_keys:
                self.sem_keys.append(sem_key)
        else:
            sem_key = eng
        o = Op(eng, fn, waits, 16 if dma else 1, len(self.ops[eng]), tag,
               sem_key)
        waits.discard(o)
        self.ops[eng].append(o)
        self.all_ops.append(o)
        for b in reads:
            b.readers.append(o)
        for b in writes:
            b.last_w = o
            b.readers = []
        return o

    def materialize(self, block, final_eng="sync", trunc=None):
        if trunc is not None:
            keep = set(id(o) for o in self.all_ops[:trunc])
            self.all_ops = self.all_ops[:trunc]
            self.ops = {e: [o for o in self.ops[e] if id(o) in keep]
                        for e in ENGS}
        for o in self.all_ops:
            if o.inc == 16:
                o.need_inc = True
            for w in o.waits:
                w.need_inc = True
        vals = {k: 0 for k in self.sem_keys}
        for o in self.all_ops:
            if o.need_inc:
                vals[o.sem_key] += o.inc
            o.val = vals[o.sem_key]
        finals = dict(vals)
        sems = {k: self.nc.semaphore("s_" + k).__enter__()
                for k in self.sem_keys}

        def emit(eng_obj, e):
            for o in self.ops[e]:
                per = {}
                for w in o.waits:
                    per[w.sem_key] = max(per.get(w.sem_key, 0), w.val)
                for we, wv in sorted(per.items()):
                    if wv > 0:
                        eng_obj.wait_ge(sems[we], wv)
                last = o.fn(eng_obj)
                if o.need_inc:
                    assert last is not None, f"op {o.tag} returned None"
                    last.then_inc(sems[o.sem_key], o.inc)
            if e == final_eng:
                for fk, fv in sorted(finals.items()):
                    if fv > 0:
                        eng_obj.wait_ge(sems[fk], fv)

        block.tensor(lambda t: emit(t, "tensor"))
        block.vector(lambda v: emit(v, "vector"))
        block.scalar(lambda s: emit(s, "scalar"))
        block.gpsimd(lambda g: emit(g, "gpsimd"))
        block.sync(lambda s: emit(s, "sync"))
        return finals


def build_program(sub_ids=(0, 1, 2, 3), trunc=None):
    nc = bass.Bass("TRN2", target_bir_lowering=False)
    din = {}

    def ein(name, shape, dtype):
        din[name] = nc.dram_tensor(name, list(shape), dtype, kind="ExternalInput").ap()
        return din[name]

    x_in = ein("x", [S, D], F32)
    y_out = nc.dram_tensor("y", [S, D], F32, kind="ExternalOutput").ap()
    for nm, shape, dt in [
        ("w_l0q", [32, 128, D], F32), ("w_l0kr", [4, 128, D], F32),
        ("w_l0dk", [4, 128, D], F32), ("w_l0upk", [4, 128, 512], F32),
        ("w_l0upv", [4, 128, 1024], F32), ("w_l0o", [32, 128, D], F32),
        ("w_l0base", [32, 128, 4224], F32), ("b_l0base", [128, 32], F32),
        ("w_l0gate", [32, 128, 9], F32), ("b_l0thr", [1, 1], F32),
        ("w_l0A", [1, 128, D], F32),
        ("w_l1q", [32, 128, D], F32), ("b_l1q", [128, 32], F32),
        ("w_l1k", [1, 128, D], F32), ("b_l1k", [128, 1], F32),
        ("w_l1v", [1, 128, D], F32), ("b_l1v", [128, 1], F32),
        ("w_l1o", [32, 128, D], F32), ("b_l1o", [128, 32], F32),
        ("w_l1base", [32, 128, 5120], F32), ("b_l1base", [128, 32], F32),
        ("w_l1A", [1, 128, 2048], F32), ("w_l1r", [32, 128, 16], F32),
        ("c_cosq", [128, S], F32), ("c_sinq", [128, S], F32),
        ("c_cosk", [128, S], F32), ("c_sink", [128, S], F32),
        ("c_maskt", [128, 2048], BF16), ("c_ident", [128, 128], BF16),
        ("c_ones", [128, 1], BF16), ("c_onesr", [1, 128], BF16),
        ("c_eexp8", [8, 128], BF16), ("c_esel", [16, 1024], BF16),
    ]:
        ein(nm, shape, dt)

    xresA = nc.dram_tensor("xresA", [S, D], F32).ap()
    xresB = nc.dram_tensor("xresB", [S, D], F32).ap()
    qdram = nc.dram_tensor("qdram", [32, 128, S], BF16).ap()
    odram = nc.dram_tensor("odram", [32, 128, S], BF16).ap()
    WC_SHAPES = {
        "w_l0q": [32, 128, D], "w_l0kr": [4, 128, D], "w_l0dk": [4, 128, D],
        "w_l0upk": [4, 128, 512], "w_l0upv": [4, 128, 1024],
        "w_l0o": [32, 128, D], "w_l0base": [32, 128, 4224],
        "w_l0A": [1, 128, D], "w_l1q": [32, 128, D], "w_l1k": [1, 128, D],
        "w_l1v": [1, 128, D], "w_l1o": [32, 128, D],
        "w_l1base": [32, 128, 5120], "w_l1A": [1, 128, 2048],
    }
    wc = {nm: nc.dram_tensor("wc_" + nm, sh, BF16).ap()
          for nm, sh in WC_SHAPES.items()}

    es = contextlib.ExitStack()
    with es:
        sbt = lambda n, sh, dt: es.enter_context(nc.sbuf_tensor(n, sh, dt))
        pst = lambda n, sh, dt: es.enter_context(nc.psum_tensor(n, sh, dt))

        afm = sbt("afm", [128, NIC * QT], BF16)
        kbuf = sbt("kbuf", [128, 16384], BF16)
        vbuf = sbt("vbuf", [128, 16384], BF16)
        wb = [sbt("wb0", [128, 5120], BF16), sbt("wb1", [128, 5120], BF16)]
        xtile = sbt("xtile", [128, D], F32)
        xout = sbt("xout", [128, D], F32)
        xhat = sbt("xhat", [128, D], BF16)
        ytmp = sbt("ytmp", [128, 1024], BF16)
        qst = sbt("qst", [128, 512], BF16)
        qbuf = sbt("qbuf", [128, 2560], BF16)
        ptb = sbt("ptb", [128, 4 * 512], BF16)
        ptm = sbt("ptm", [128, 512], BF16)
        scb = sbt("scb", [128, D], BF16)
        shf = sbt("shf", [128, 1024], F32)
        strow = sbt("strow", [1, 512], F32)
        rsbf = sbt("rsbf", [1, 512], BF16)
        wfm = sbt("wfm", [16, 512], BF16)
        bias = sbt("bias", [128, 96], F32)
        st = sbt("st", [128, 128], F32)
        bthr = sbt("bthr", [1, 16], BF16)
        bthf = sbt("bthf", [1, 1], F32)
        cst = {}
        for nm, shp in (("c_ident", [128, 128]), ("c_ones", [128, 1]),
                        ("c_onesr", [1, 128]), ("c_eexp8", [8, 128]),
                        ("c_esel", [16, 1024])):
            cst[nm] = sbt("s" + nm, shp, BF16)

        mm = [pst(f"mm{i}", [128, 512], F32) for i in range(6)]
        tp = pst("tp", [128, 1024], BF16)
        smp = pst("smp", [128, 512], F32)

        P = Prog(nc)
        bafm, bkbuf, bvbuf = Buf("afm"), Buf("kbuf"), Buf("vbuf")
        bwb = [Buf("wb0"), Buf("wb1")]
        bxtile, bxout, bxhat = Buf("xtile"), Buf("xout"), Buf("xhat")
        bytmp, bqst, bqbuf = Buf("ytmp"), Buf("qst"), Buf("qbuf")
        bptb = [Buf(f"ptb{i}") for i in range(4)]
        bptm, bscb, bshf = Buf("ptm"), Buf("scb"), Buf("shf")
        bstrow, brsbf, bwfm = Buf("strow"), Buf("rsbf"), Buf("wfm")
        bbias, bst, bbthr = Buf("bias"), Buf("st"), Buf("bthr")
        bmm = [Buf(f"mm{i}") for i in range(6)]
        btp0 = Buf("tp")
        btp = [btp0, btp0]
        bsmp = Buf("smp")
        bqd = [Buf(f"qdram{i}") for i in range(NQ)]
        bod = [Buf(f"odram{i}") for i in range(NQ)]
        bcst = Buf("cst")
        bxA = [Buf(f"xresA{i}") for i in range(NQ)]
        bxB = [Buf(f"xresB{i}") for i in range(NQ)]
        bY = [Buf(f"y{i}") for i in range(NQ)]
        bX = [Buf(f"x{i}") for i in range(NQ)]
        bwc = {nm: Buf("wc_" + nm) for nm in WC_SHAPES}

        def dma(engq, dst, src, reads, writes, tag=""):
            return P.op(engq, lambda e, d=dst, s=src: e.dma_start(out=d, in_=s),
                        reads=reads, writes=writes, dma=True, tag=tag)

        for nm in cst:
            dma("sync", cst[nm][:, :], din[nm][:, :], [], [bcst], tag="cst")
        dma("sync", scb[:, 2048:4096], din["c_maskt"][:, :], [], [bscb], tag="mask")
        for nm in WC_SHAPES:
            dma("gpsimd", wc[nm][:, :, :], din[nm][:, :, :], [], [bwc[nm]],
                tag="wcast_" + nm)

        def stc(a, c):
            return st[:, a + c:a + c + 1]

        def ln_stats(area, src_ap, src_buf, tag):
            P.op("vector", lambda e, s=src_ap: e.reduce_sum(stc(area, 0), s, AX),
                 reads=[src_buf], writes=[bst], tag=tag + "s")
            P.op("scalar", lambda e, s=src_ap: e.activation(
                xhat[:, :], s, AF.Square, accum_out=stc(area, 1)),
                 reads=[src_buf], writes=[bxhat, bst], tag=tag + "q")
            P.op("vector", lambda e: e.tensor_scalar(
                stc(area, 2), stc(area, 0), -1.0 / D, None, ALU.mult),
                 reads=[bst], writes=[bst], tag=tag + "m")
            P.op("vector", lambda e: e.tensor_tensor(
                stc(area, 3), stc(area, 2), stc(area, 2), ALU.mult),
                 reads=[bst], writes=[bst], tag=tag + "2")
            P.op("vector", lambda e: e.scalar_tensor_tensor(
                stc(area, 4), stc(area, 1), 1.0 / D, stc(area, 3),
                ALU.mult, ALU.subtract),
                 reads=[bst], writes=[bst], tag=tag + "v")
            P.op("vector", lambda e: e.tensor_scalar(
                stc(area, 8), stc(area, 4), EPS, None, ALU.add),
                 reads=[bst], writes=[bst], tag=tag + "e")
            P.op("scalar", lambda e: e.activation(
                stc(area, 5), stc(area, 8), AF.Ln),
                 reads=[bst], writes=[bst], tag=tag + "l")
            P.op("scalar", lambda e: e.activation(
                stc(area, 6), stc(area, 5), AF.Exp, scale=-0.5),
                 reads=[bst], writes=[bst], tag=tag + "r")
            P.op("vector", lambda e: e.tensor_tensor(
                stc(area, 7), stc(area, 2), stc(area, 6), ALU.mult),
                 reads=[bst], writes=[bst], tag=tag + "n")
            return stc(area, 6), stc(area, 7)

        def emit_ln1(q, xin_d, bxin):
            for tq in range(4):
                tt = q * 4 + tq
                dma("sync", xtile[:, :], xin_d[tt * 128:(tt + 1) * 128, :],
                    [bxin[q]], [bxtile], tag=f"l1x{tt}")
                rstd, nmr = ln_stats(0, xtile[:, :], bxtile, f"l1s{tt}")
                P.op("vector", lambda e, r=rstd, n=nmr: e.tensor_scalar(
                    xhat[:, :], xtile[:, :], r, n, ALU.mult, ALU.add),
                     reads=[bxtile, bst], writes=[bxhat], tag=f"l1a{tt}")
                for b in range(8):
                    half = (b % 2) * 512

                    def tr(e, b=b, half=half):
                        r = None
                        for j in range(4):
                            r = e.transpose(
                                tp[:, half + j * 128:half + (j + 1) * 128],
                                xhat[:, (b * 4 + j) * 128:(b * 4 + j + 1) * 128],
                                cst["c_ident"][:, :])
                        return r
                    P.op("tensor", tr, reads=[bxhat, bcst], writes=[btp[b % 2]],
                         tag=f"l1t{tt}_{b}")
                    dst = afm[:, :].rearrange("p (ic t) -> p ic t", t=QT)[
                        :, b * 4:(b + 1) * 4, tq * 128:(tq + 1) * 128]
                    src = tp[:, half:half + 512].rearrange("p (j t) -> p j t", t=128)
                    P.op("vector", lambda e, d=dst, s=src: e.tensor_copy(d, s),
                         reads=[btp[b % 2]], writes=[bafm], tag=f"l1e{tt}_{b}")

        def rhs_afm(ic):
            return afm[:, ic * QT:(ic + 1) * QT]

        def rope_evac(bank, cos_off, sin_off, dsts, dstbuf, tag):
            P.op("vector", lambda e: e.tensor_tensor(
                xout[:, 2048:2560], mm[bank][:, 0:QT],
                xout[:, cos_off:cos_off + 512], ALU.mult),
                 reads=[bmm[bank], bxout], writes=[bxout], tag=tag + "c")

            def terms(e):
                r = None
                for (o0, s0) in ((0, 32), (32, 0), (64, 96), (96, 64)):
                    r = e.tensor_tensor(
                        xout[o0:o0 + 32, 2560:3072],
                        mm[bank][s0:s0 + 32, 0:QT],
                        xout[o0:o0 + 32, sin_off:sin_off + 512], ALU.mult)
                return r
            P.op("vector", terms, reads=[bmm[bank], bxout], writes=[bxout],
                 tag=tag + "t")
            for (d, r0, nr) in dsts:
                P.op("vector", lambda e, d=d, r0=r0, nr=nr: e.tensor_tensor(
                    d, xout[r0:r0 + nr, 2048:2560], xout[r0:r0 + nr, 2560:3072],
                    ALU.add),
                     reads=[bxout], writes=[dstbuf], tag=tag + "a")

        def proj_fm(q, wname, n_oc, n_ic, co, wwidth, evac, tag):
            wt_d = wc[wname]
            for oc in range(n_oc):
                wi = oc % 2
                dma("sync" if wi == 0 else "gpsimd",
                    wb[wi][:, 0:wwidth], wt_d[oc, :, :],
                    [bwc[wname]], [bwb[wi]], tag=f"{tag}w{oc}")
                bank = oc % 4

                def grp(e, wi=wi, bank=bank):
                    r = None
                    for ic in range(n_ic):
                        r = e.matmul(mm[bank][:, 0:QT],
                                     wb[wi][:, ic * co:(ic + 1) * co],
                                     rhs_afm(ic), start=(ic == 0),
                                     stop=(ic == n_ic - 1))
                    return r
                P.op("tensor", grp, reads=[bwb[wi], bafm], writes=[bmm[bank]],
                     tag=f"{tag}m{oc}")
                evac(q, oc, bank)

        def emit_attention_group(l0, qtg):
            ktmax = 4 * (qtg + 1)
            for h in range(32):
                g = (h // 4) if l0 else 0
                kvcol = g * S
                dma("sync", qbuf[:, qtg * QT:(qtg + 1) * QT],
                    qdram[h, :, qtg * QT:(qtg + 1) * QT], [bqd[qtg]], [bqbuf],
                    tag=f"ql{h}_{qtg}")
                for kt in range(ktmax):
                    stb = kt % 2
                    lhs = kbuf[:, kvcol + kt * 128:kvcol + (kt + 1) * 128]
                    rhs = qbuf[:, qtg * QT:(qtg + 1) * QT]
                    P.op("tensor", lambda e, l=lhs, r=rhs, stb=stb: e.matmul(
                        mm[stb][:, 0:QT], l, r, start=True, stop=True,
                        skip_group_check=True),
                         reads=[bkbuf, bqbuf], writes=[bmm[stb]],
                         tag=f"st{h}_{kt}")
                    pb = kt % 4
                    pt_ap = ptb[:, pb * 512:(pb + 1) * 512]
                    P.op("scalar", lambda e, stb=stb, p=pt_ap: e.activation(
                        p, mm[stb][:, 0:QT], AF.Exp),
                         reads=[bmm[stb]], writes=[bptb[pb]], tag=f"ex{h}_{kt}")
                    psrc, psb = pt_ap, bptb[pb]
                    if kt >= qtg * 4:
                        j = kt - qtg * 4
                        msk = scb[:, 2048 + j * 512:2048 + (j + 1) * 512]
                        P.op("vector", lambda e, p=psrc, m=msk: e.tensor_tensor(
                            ptm[:, :], p, m, ALU.mult),
                             reads=[psb, bscb], writes=[bptm], tag=f"mk{h}_{kt}")
                        psrc, psb = ptm[:, :], bptm
                    sb_t, sb_b = (smp, bsmp) if h % 2 == 0 else (mm[5], bmm[5])
                    P.op("tensor", lambda e, p=psrc, kt=kt, sb=sb_t: e.matmul(
                        sb[0:1, 0:QT], cst["c_ones"][:, :], p,
                        start=(kt == 0), stop=(kt == ktmax - 1),
                        skip_group_check=True),
                         reads=[psb, bcst], writes=[sb_b], tag=f"sm{h}_{kt}")
                    ob = 2 + (h % 2)
                    voff = kt * 1024 + (g * 128 if l0 else 0)
                    vap = vbuf[:, voff:voff + 128]
                    P.op("tensor", lambda e, p=psrc, v=vap, kt=kt, ob=ob: e.matmul(
                        mm[ob][:, 0:QT], v, p,
                        start=(kt == 0), stop=(kt == ktmax - 1),
                        skip_group_check=True),
                         reads=[psb, bvbuf], writes=[bmm[ob]], tag=f"av{h}_{kt}")
                hp = h % 2
                P.op("vector", lambda e, sb=sb_t, hp=hp: e.reciprocal(
                    strow[0:1, hp * 256:hp * 256 + 256] if False else strow[0:1, :],
                    sb[0:1, 0:QT]),
                     reads=[sb_b], writes=[bstrow], tag=f"rc{h}")
                P.op("vector", lambda e: e.tensor_copy(rsbf[0:1, :], strow[0:1, :]),
                     reads=[bstrow], writes=[brsbf], tag=f"rb{h}")
                P.op("tensor", lambda e, hp=hp: e.matmul(
                    mm[4][:, 0:QT] if hp == 0 else mm[(0) if False else 4][:, 0:QT],
                    cst["c_onesr"][:, :], rsbf[0:1, :],
                    start=True, stop=True, skip_group_check=True),
                     reads=[brsbf, bcst], writes=[bmm[4]], tag=f"bc{h}")
                P.op("vector", lambda e, hp=hp: e.tensor_copy(
                    shf[:, hp * 512:(hp + 1) * 512], mm[4][:, 0:QT]),
                     reads=[bmm[4]], writes=[bshf], tag=f"be{h}")
                ob = 2 + (h % 2)
                P.op("vector", lambda e, hp=hp, ob=ob: e.tensor_tensor(
                    ytmp[:, hp * 512:(hp + 1) * 512], mm[ob][:, 0:QT],
                    shf[:, hp * 512:(hp + 1) * 512], ALU.mult),
                     reads=[bmm[ob], bshf], writes=[bytmp], tag=f"oe{h}")
                dma("scalar", odram[h, :, qtg * QT:(qtg + 1) * QT],
                    ytmp[:, hp * 512:(hp + 1) * 512], [bytmp], [bod[qtg]],
                    tag=f"os{h}")

        def final_evac_chain(q, oc, bank, bias_col):
            if bias_col is not None:
                P.op("vector", lambda e, oc=oc, bank=bank, bc=bias_col:
                     e.tensor_scalar(ytmp[:, 0:512], mm[bank][:, 0:QT],
                                     bias[:, bc + oc:bc + oc + 1], None, ALU.add),
                     reads=[bmm[bank], bbias], writes=[bytmp], tag=f"fe{oc}")
            else:
                P.op("vector", lambda e, bank=bank: e.tensor_copy(
                    ytmp[:, 0:512], mm[bank][:, 0:QT]),
                     reads=[bmm[bank]], writes=[bytmp], tag=f"fe{oc}")

            def tr(e):
                r = None
                for j in range(4):
                    r = e.transpose(tp[:, j * 128:(j + 1) * 128],
                                    ytmp[:, j * 128:(j + 1) * 128],
                                    cst["c_ident"][:, :])
                return r
            P.op("tensor", tr, reads=[bytmp, bcst], writes=[btp[0]],
                 tag=f"ft{oc}")
            dst = vbuf[:, :].rearrange("p (tq c) -> p tq c", c=D)[
                :, :, oc * 128:(oc + 1) * 128]
            src = tp[:, 0:512].rearrange("p (j t) -> p j t", t=128)
            P.op("vector", lambda e, d=dst, s=src: e.tensor_copy(d, s),
                 reads=[btp[0]], writes=[bvbuf], tag=f"fv{oc}")

        def emit_ln2_res(q, xin_d, bxin, xout_d, bxo, tag):
            for tq in range(4):
                tt = q * 4 + tq
                ytile = vbuf[:, tq * D:(tq + 1) * D]
                rstd, nmr = ln_stats(32, ytile, bvbuf, f"{tag}2{tt}")
                dma("sync", xtile[:, :], xin_d[tt * 128:(tt + 1) * 128, :],
                    [bxin[q]], [bxtile], tag=f"{tag}x{tt}")
                P.op("vector", lambda e, n=nmr: e.tensor_scalar(
                    xout[:, :], xtile[:, :], n, None, ALU.add),
                     reads=[bxtile, bst], writes=[bxout], tag=f"{tag}a{tt}")
                P.op("vector", lambda e, r=rstd, y=ytile: e.scalar_tensor_tensor(
                    xtile[:, :], y, r, xout[:, :], ALU.mult, ALU.add),
                     reads=[bvbuf, bst, bxout], writes=[bxtile], tag=f"{tag}b{tt}")
                dma("sync", xout_d[tt * 128:(tt + 1) * 128, :], xtile[:, :],
                    [bxtile], [bxo[q]], tag=f"{tag}o{tt}")

        def emit_final_q(q, xin_d, bxin, xout_d, bxo, wname, wwidth, extra,
                         extra_reads, bias_col, tag):
            wt_d = wc[wname]
            for oc in range(32):
                wi = oc % 2
                dma("sync" if wi == 0 else "gpsimd",
                    wb[wi][:, 0:wwidth], wt_d[oc, :, :],
                    [bwc[wname]], [bwb[wi]], tag=f"{tag}w{oc}")
                bank = oc % 4

                def grp(e, wi=wi, bank=bank):
                    r = None
                    for ic in range(NIC):
                        r = e.matmul(mm[bank][:, 0:QT],
                                     wb[wi][:, ic * 128:(ic + 1) * 128],
                                     rhs_afm(ic), start=(ic == 0),
                                     stop=(ic == NIC - 1 and extra is None))
                    if extra is not None:
                        r = extra(e, wi, bank)
                    return r
                P.op("tensor", grp,
                     reads=[bwb[wi], bafm] + list(extra_reads),
                     writes=[bmm[bank]], tag=f"{tag}m{oc}")
                final_evac_chain(q, oc, bank, bias_col)
            emit_ln2_res(q, xin_d, bxin, xout_d, bxo, tag)

        def emit_l0_gate(q):
            A = 64
            for tb in range(4):
                def g_g(e, tb=tb):
                    r = None
                    for ic in range(NIC):
                        r = e.matmul(
                            mm[4][:, 0:9],
                            afm[:, ic * QT + tb * 128:ic * QT + (tb + 1) * 128],
                            qbuf[:, ic * 9:(ic + 1) * 9],
                            start=(ic == 0), stop=False, skip_group_check=True)
                    return e.matmul(mm[4][:, 0:9], cst["c_onesr"][:, :],
                                    bthr[0:1, 0:9], start=False, stop=True,
                                    skip_group_check=True)
                P.op("tensor", g_g, reads=[bafm, bqbuf, bbthr, bcst],
                     writes=[bmm[4]], tag=f"gg{tb}")
                P.op("vector", lambda e: e.reduce_max(
                    stc(A, 0), mm[4][:, 0:8], AX, negate=True),
                     reads=[bmm[4]], writes=[bst], tag="gm")
                P.op("scalar", lambda e: e.activation(
                    st[:, 80:88], mm[4][:, 0:8], AF.Exp, bias=stc(A, 0),
                    accum_out=stc(A, 1)),
                     reads=[bmm[4], bst], writes=[bst], tag="ge")
                P.op("scalar", lambda e: e.activation(
                    stc(A, 2), mm[4][:, 8:9], AF.Exp, scale=-1.0),
                     reads=[bmm[4]], writes=[bst], tag="gs")
                P.op("vector", lambda e: e.tensor_scalar(
                    stc(A, 3), stc(A, 2), 1.0, None, ALU.add),
                     reads=[bst], writes=[bst], tag="g1")
                P.op("vector", lambda e: e.reciprocal(stc(A, 4), stc(A, 3)),
                     reads=[bst], writes=[bst], tag="g2")
                P.op("vector", lambda e: e.tensor_scalar(
                    stc(A, 5), stc(A, 4), -0.125, None, ALU.mult),
                     reads=[bst], writes=[bst], tag="g3")
                P.op("vector", lambda e: e.reciprocal(stc(A, 6), stc(A, 1)),
                     reads=[bst], writes=[bst], tag="g4")
                P.op("vector", lambda e: e.tensor_scalar(
                    st[:, 88:96], st[:, 80:88], stc(A, 6), stc(A, 5),
                    ALU.mult, ALU.add),
                     reads=[bst], writes=[bst], tag="g5")
                P.op("vector", lambda e: e.tensor_scalar(
                    st[:, 96:104], st[:, 88:96], 0.0, None, ALU.max),
                     reads=[bst], writes=[bst], tag="g6")
                P.op("vector", lambda e: e.reduce_sum(
                    stc(A, 7), st[:, 96:104], AX),
                     reads=[bst], writes=[bst], tag="g7")
                P.op("vector", lambda e: e.tensor_scalar(
                    stc(A, 8), stc(A, 7), 0.0, None, ALU.is_equal),
                     reads=[bst], writes=[bst], tag="g8")
                P.op("vector", lambda e: e.tensor_tensor(
                    stc(A, 9), stc(A, 7), stc(A, 8), ALU.add),
                     reads=[bst], writes=[bst], tag="g9")
                P.op("vector", lambda e: e.reciprocal(stc(A, 10), stc(A, 9)),
                     reads=[bst], writes=[bst], tag="gA")
                P.op("vector", lambda e: e.tensor_scalar(
                    stc(A, 11), stc(A, 10), 2.0, None, ALU.mult),
                     reads=[bst], writes=[bst], tag="gB")
                P.op("vector", lambda e, tb=tb: e.tensor_scalar(
                    ytmp[:, tb * 8:(tb + 1) * 8], st[:, 96:104], stc(A, 11),
                    None, ALU.mult),
                     reads=[bst], writes=[bytmp], tag="gC")
                P.op("tensor", lambda e, tb=tb: e.transpose(
                    tp[0:8, 512 + tb * 128:512 + (tb + 1) * 128],
                    ytmp[:, tb * 8:(tb + 1) * 8], cst["c_ident"][:, :]),
                     reads=[bytmp, bcst], writes=[btp[1]], tag="gD")
            P.op("vector", lambda e: e.tensor_copy(wfm[0:8, 0:512],
                                                   tp[0:8, 512:1024]),
                 reads=[btp[1]], writes=[bwfm], tag="gE")
            P.op("tensor", lambda e: e.matmul(
                mm[4][:, 0:QT], cst["c_eexp8"][:, :], wfm[0:8, 0:512],
                start=True, stop=True, skip_group_check=True),
                 reads=[bwfm, bcst], writes=[bmm[4]], tag="gF")
            P.op("vector", lambda e: e.tensor_copy(shf[:, 0:512], mm[4][:, 0:QT]),
                 reads=[bmm[4]], writes=[bshf], tag="gG")

        def emit_l1_route(q):
            A = 64

            def g_rh(e):
                r = None
                for ic in range(NIC):
                    r = e.matmul(mm[5][0:64, 0:QT],
                                 qbuf[:, ic * 64:(ic + 1) * 64],
                                 rhs_afm(ic), start=(ic == 0),
                                 stop=(ic == NIC - 1))
                return r
            P.op("tensor", g_rh, reads=[bqbuf, bafm], writes=[bmm[5]], tag="rh")
            P.op("vector", lambda e: e.tensor_copy(shf[0:64, 0:512],
                                                   mm[5][0:64, 0:QT]),
                 reads=[bmm[5]], writes=[bshf], tag="rhe")
            for tb in range(4):
                def g_r(e, tb=tb):
                    r = None
                    for ic in range(NIC):
                        r = e.matmul(
                            mm[4][:, 0:16],
                            afm[:, ic * QT + tb * 128:ic * QT + (tb + 1) * 128],
                            qbuf[:, 2048 + ic * 16:2048 + (ic + 1) * 16],
                            start=(ic == 0), stop=(ic == NIC - 1),
                            skip_group_check=True)
                    return r
                P.op("tensor", g_r, reads=[bafm, bqbuf], writes=[bmm[4]],
                     tag=f"rr{tb}")
                P.op("vector", lambda e: e.reduce_max(
                    stc(A, 0), mm[4][:, 0:16], AX, negate=True),
                     reads=[bmm[4]], writes=[bst], tag="t0")
                P.op("scalar", lambda e: e.activation(
                    st[:, 112:128], mm[4][:, 0:16], AF.Exp, bias=stc(A, 0),
                    accum_out=stc(A, 1)),
                     reads=[bmm[4], bst], writes=[bst], tag="t1")
                P.op("vector", lambda e: e.reciprocal(stc(A, 2), stc(A, 1)),
                     reads=[bst], writes=[bst], tag="t2")
                P.op("vector", lambda e: e.tensor_scalar(
                    st[:, 80:96], st[:, 112:128], stc(A, 2), None, ALU.mult),
                     reads=[bst], writes=[bst], tag="t3")
                P.op("vector", lambda e: e.reduce_max(stc(A, 3), st[:, 80:96], AX),
                     reads=[bst], writes=[bst], tag="t4")
                P.op("vector", lambda e: e.tensor_scalar(
                    st[:, 96:112], st[:, 80:96], stc(A, 3), None, ALU.is_ge),
                     reads=[bst], writes=[bst], tag="t5")
                P.op("vector", lambda e: e.scalar_tensor_tensor(
                    st[:, 112:128], st[:, 96:112], -2.0, st[:, 80:96],
                    ALU.mult, ALU.add),
                     reads=[bst], writes=[bst], tag="t6")
                P.op("vector", lambda e: e.reduce_max(stc(A, 4), st[:, 112:128], AX),
                     reads=[bst], writes=[bst], tag="t7")
                P.op("vector", lambda e: e.tensor_tensor(
                    stc(A, 5), stc(A, 3), stc(A, 4), ALU.add),
                     reads=[bst], writes=[bst], tag="t8")
                P.op("vector", lambda e: e.tensor_scalar(
                    stc(A, 6), stc(A, 5), 1e-6, None, ALU.add),
                     reads=[bst], writes=[bst], tag="t9")
                P.op("vector", lambda e: e.reciprocal(stc(A, 7), stc(A, 6)),
                     reads=[bst], writes=[bst], tag="tA")
                P.op("vector", lambda e: e.tensor_scalar(
                    stc(A, 8), stc(A, 7), 0.5, None, ALU.mult),
                     reads=[bst], writes=[bst], tag="tB")
                P.op("vector", lambda e: e.tensor_tensor(
                    stc(A, 9), stc(A, 3), stc(A, 8), ALU.mult),
                     reads=[bst], writes=[bst], tag="tC")
                P.op("vector", lambda e: e.tensor_tensor(
                    stc(A, 10), stc(A, 4), stc(A, 8), ALU.mult),
                     reads=[bst], writes=[bst], tag="tD")
                P.op("vector", lambda e: e.tensor_scalar(
                    st[:, 80:96], st[:, 112:128], stc(A, 4), None, ALU.is_ge),
                     reads=[bst], writes=[bst], tag="tE")
                P.op("vector", lambda e: e.tensor_scalar(
                    st[:, 112:128], st[:, 96:112], stc(A, 9), None, ALU.mult),
                     reads=[bst], writes=[bst], tag="tF")
                P.op("vector", lambda e, tb=tb: e.scalar_tensor_tensor(
                    ytmp[:, tb * 16:(tb + 1) * 16], st[:, 80:96], stc(A, 10),
                    st[:, 112:128], ALU.mult, ALU.add),
                     reads=[bst], writes=[bytmp], tag="tG")
                P.op("tensor", lambda e, tb=tb: e.transpose(
                    tp[0:16, 512 + tb * 128:512 + (tb + 1) * 128],
                    ytmp[:, tb * 16:(tb + 1) * 16], cst["c_ident"][:, :]),
                     reads=[bytmp, bcst], writes=[btp[1]], tag="tH")
            P.op("vector", lambda e: e.tensor_copy(wfm[0:16, 0:512],
                                                   tp[0:16, 512:1024]),
                 reads=[btp[1]], writes=[bwfm], tag="tI")
            for cc in range(8):
                P.op("tensor", lambda e, cc=cc: e.matmul(
                    mm[4][:, 0:QT], cst["c_esel"][:, cc * 128:(cc + 1) * 128],
                    wfm[0:16, 0:512], start=True, stop=True,
                    skip_group_check=True),
                     reads=[bwfm, bcst], writes=[bmm[4]], tag=f"cp{cc}")
                P.op("vector", lambda e, cc=cc: e.tensor_tensor(
                    scb[0:64, cc * 512:(cc + 1) * 512], mm[4][0:64, 0:QT],
                    shf[0:64, 0:512], ALU.mult),
                     reads=[bmm[4], bshf], writes=[bscb], tag=f"ca{cc}")
                P.op("vector", lambda e, cc=cc: e.tensor_tensor(
                    scb[64:128, cc * 512:(cc + 1) * 512], mm[4][64:128, 0:QT],
                    shf[0:64, 0:512], ALU.mult),
                     reads=[bmm[4], bshf], writes=[bscb], tag=f"cb{cc}")

        def sub_attn(layer, xin_d, bxin, xout_d, bxo):
            l0 = (layer == 0)
            if not l0:
                dma("sync", bias[:, 0:32], din["b_l1q"][:, :], [], [bbias], tag="bq")
                dma("sync", bias[:, 64:65], din["b_l1k"][:, :], [], [bbias], tag="bk")
                dma("sync", bias[:, 65:66], din["b_l1v"][:, :], [], [bbias], tag="bv")
                dma("sync", bias[:, 32:64], din["b_l1o"][:, :], [], [bbias], tag="bo")
            for q in range(NQ):
                emit_ln1(q, xin_d, bxin)
                if l0:
                    for off, nm in ((0, "c_cosq"), (512, "c_sinq"),
                                    (1024, "c_cosk"), (1536, "c_sink")):
                        dma("sync", xout[:, off:off + 512],
                            din[nm][:, q * QT:(q + 1) * QT], [], [bxout],
                            tag=f"tb{off}")

                    def ev_q(q_, oc, bank):
                        rope_evac(bank, 0, 512, [(qst[:, 0:512], 0, 128)],
                                  bqst, f"rq{oc}")
                        dma("scalar", qdram[oc, :, q_ * QT:(q_ + 1) * QT],
                            qst[:, 0:512], [bqst], [bqd[q_]], tag=f"qs{oc}")
                    proj_fm(q, "w_l0q", 32, NIC, 128, D, ev_q, "pq")

                    def ev_ckv(q_, oc, bank):
                        P.op("vector", lambda e, oc=oc, bank=bank: e.tensor_copy(
                            scb[:, oc * 512:(oc + 1) * 512], mm[bank][:, 0:QT]),
                             reads=[bmm[bank]], writes=[bscb], tag=f"ck{oc}")
                    proj_fm(q, "w_l0dk", 4, NIC, 128, D, ev_ckv, "pd")

                    def ev_kr(q_, oc, bank):
                        h0, h1 = 2 * oc, 2 * oc + 1
                        rope_evac(bank, 1024, 1536,
                                  [(kbuf[0:64, h0 * S + q_ * QT:h0 * S + (q_ + 1) * QT], 0, 64),
                                   (kbuf[0:64, h1 * S + q_ * QT:h1 * S + (q_ + 1) * QT], 64, 64)],
                                  bkbuf, f"rk{oc}")
                    proj_fm(q, "w_l0kr", 4, NIC, 128, D, ev_kr, "pk")

                    for oc in range(4):
                        wi = oc % 2
                        dma("sync" if wi == 0 else "gpsimd",
                            wb[wi][:, 0:512], wc["w_l0upk"][oc, :, :],
                            [bwc["w_l0upk"]], [bwb[wi]], tag=f"uk{oc}")

                        def g_upk(e, wi=wi):
                            r = None
                            for icc in range(4):
                                r = e.matmul(mm[4][:, 0:QT],
                                             wb[wi][:, icc * 128:(icc + 1) * 128],
                                             scb[:, icc * 512:(icc + 1) * 512],
                                             start=(icc == 0), stop=(icc == 3))
                            return r
                        P.op("tensor", g_upk, reads=[bwb[wi], bscb],
                             writes=[bmm[4]], tag=f"ku{oc}")
                        d0 = kbuf[64:128, 2 * oc * S + q * QT:2 * oc * S + (q + 1) * QT]
                        d1 = kbuf[64:128, (2 * oc + 1) * S + q * QT:(2 * oc + 1) * S + (q + 1) * QT]
                        P.op("vector", lambda e, d=d0: e.tensor_copy(
                            d, mm[4][0:64, 0:QT]),
                             reads=[bmm[4]], writes=[bkbuf], tag=f"ka{oc}")
                        P.op("vector", lambda e, d=d1: e.tensor_copy(
                            d, mm[4][64:128, 0:QT]),
                             reads=[bmm[4]], writes=[bkbuf], tag=f"kb{oc}")

                    dma("sync", wb[0][:, 0:4096].rearrange("p (c o) -> p c o", o=1024),
                        wc["w_l0upv"][:, :, :].rearrange("c p o -> p c o"),
                        [bwc["w_l0upv"]], [bwb[0]], tag="uv")
                    for j in range(4):
                        ktt = q * 4 + j
                        for dc in range(2):
                            def g_v(e, j=j, dc=dc):
                                r = None
                                for icc in range(4):
                                    r = e.matmul(
                                        mm[5][:, 0:QT],
                                        scb[:, icc * 512 + j * 128:icc * 512 + (j + 1) * 128],
                                        wb[0][:, icc * 1024 + dc * 512:icc * 1024 + (dc + 1) * 512],
                                        start=(icc == 0), stop=(icc == 3))
                                return r
                            P.op("tensor", g_v, reads=[bwb[0], bscb],
                                 writes=[bmm[5]], tag=f"v{j}{dc}")
                            dv = vbuf[:, ktt * 1024 + dc * 512:ktt * 1024 + (dc + 1) * 512]
                            P.op("vector", lambda e, d=dv: e.tensor_copy(
                                d, mm[5][:, 0:QT]),
                                 reads=[bmm[5]], writes=[bvbuf], tag=f"w{j}{dc}")
                else:
                    def ev_q1(q_, oc, bank):
                        P.op("vector", lambda e, oc=oc, bank=bank: e.tensor_scalar(
                            qst[:, 0:512], mm[bank][:, 0:QT],
                            bias[:, oc:oc + 1], 1.0 / np.sqrt(128.0),
                            ALU.add, ALU.mult),
                             reads=[bmm[bank], bbias], writes=[bqst], tag=f"qe{oc}")
                        dma("scalar", qdram[oc, :, q_ * QT:(q_ + 1) * QT],
                            qst[:, 0:512], [bqst], [bqd[q_]], tag=f"qs{oc}")
                    proj_fm(q, "w_l1q", 32, NIC, 128, D, ev_q1, "p1")

                    def ev_k1(q_, oc, bank):
                        dk = kbuf[:, q_ * QT:(q_ + 1) * QT]
                        P.op("vector", lambda e, d=dk, bank=bank: e.tensor_scalar(
                            d, mm[bank][:, 0:QT], bias[:, 64:65], None, ALU.add),
                             reads=[bmm[bank], bbias], writes=[bkbuf], tag="ke")
                    proj_fm(q, "w_l1k", 1, NIC, 128, D, ev_k1, "k1")

                    def ev_v1(q_, oc, bank):
                        P.op("vector", lambda e, bank=bank: e.tensor_scalar(
                            ytmp[:, 0:512], mm[bank][:, 0:QT],
                            bias[:, 65:66], None, ALU.add),
                             reads=[bmm[bank], bbias], writes=[bytmp], tag="vE")

                        def trv(e):
                            r = None
                            for j in range(4):
                                r = e.transpose(tp[:, j * 128:(j + 1) * 128],
                                                ytmp[:, j * 128:(j + 1) * 128],
                                                cst["c_ident"][:, :])
                            return r
                        P.op("tensor", trv, reads=[bytmp, bcst], writes=[btp[0]],
                             tag="vT")
                        dvv = vbuf[:, :].rearrange("p (kt c) -> p kt c", c=1024)[
                            :, q_ * 4:(q_ + 1) * 4, 0:128]
                        sv = tp[:, 0:512].rearrange("p (j t) -> p j t", t=128)
                        P.op("vector", lambda e, d=dvv, s=sv: e.tensor_copy(d, s),
                             reads=[btp[0]], writes=[bvbuf], tag="vC")
                    proj_fm(q, "w_l1v", 1, NIC, 128, D, ev_v1, "v1")

                emit_attention_group(l0, q)

            wname = "w_l0o" if l0 else "w_l1o"
            wt = wc[wname]
            bias_col = None if l0 else 32
            for q in range(NQ):
                src = odram[:, :, q * QT:(q + 1) * QT].rearrange("h p t -> p h t")
                dst = afm[:, :].rearrange("p (ic t) -> p ic t", t=QT)
                dma("sync", dst, src, [bod[q]], [bafm], tag=f"ol{q}")
                for oc in range(32):
                    wi = oc % 2
                    dma("sync" if wi == 0 else "gpsimd",
                        wb[wi][:, 0:D], wt[oc, :, :], [bwc[wname]], [bwb[wi]],
                        tag=f"ow{oc}")
                    bank = oc % 4

                    def grp(e, wi=wi, bank=bank):
                        r = None
                        for ic in range(NIC):
                            r = e.matmul(mm[bank][:, 0:QT],
                                         wb[wi][:, ic * 128:(ic + 1) * 128],
                                         rhs_afm(ic), start=(ic == 0),
                                         stop=(ic == NIC - 1))
                        return r
                    P.op("tensor", grp, reads=[bwb[wi], bafm],
                         writes=[bmm[bank]], tag=f"om{oc}")
                    final_evac_chain(q, oc, bank, bias_col)
                emit_ln2_res(q, xin_d, bxin, xout_d, bxo, "wo")

        def sub_ffn(layer, xin_d, bxin, xout_d, bxo):
            l0 = (layer == 0)
            if l0:
                dma("sync", bias[:, 0:32], din["b_l0base"][:, :], [], [bbias],
                    tag="bb")
                dma("sync", bthf[0:1, 0:1], din["b_l0thr"][:, :], [], [bbthr],
                    tag="bt")
                P.op("vector", lambda e: e.memset(bthr[0:1, 0:16], 0.0),
                     reads=[], writes=[bbthr], tag="bz")
                P.op("vector", lambda e: e.tensor_copy(bthr[0:1, 8:9],
                                                       bthf[0:1, 0:1]),
                     reads=[bbthr], writes=[bbthr], tag="bc")
                dma("gpsimd", qbuf[:, 0:288].rearrange("p (c o) -> p c o", o=9),
                    din["w_l0gate"][:, :, :].rearrange("c p o -> p c o"),
                    [], [bqbuf], tag="gw")
            else:
                dma("sync", bias[:, 0:32], din["b_l1base"][:, :], [], [bbias],
                    tag="bb")
                dma("sync", qbuf[:, 0:2048], wc["w_l1A"][0, :, :],
                    [bwc["w_l1A"]], [bqbuf], tag="aw")
                dma("gpsimd", qbuf[:, 2048:2560].rearrange("p (c o) -> p c o", o=16),
                    din["w_l1r"][:, :, :].rearrange("c p o -> p c o"),
                    [], [bqbuf], tag="rw")
            for q in range(NQ):
                emit_ln1(q, xin_d, bxin)
                if l0:
                    emit_l0_gate(q)
                    dma("sync", wb[0][:, 0:D], wc["w_l0A"][0, :, :],
                        [bwc["w_l0A"]], [bwb[0]], tag="ha")

                    def g_h(e):
                        r = None
                        for ic in range(NIC):
                            r = e.matmul(mm[5][:, 0:QT],
                                         wb[0][:, ic * 128:(ic + 1) * 128],
                                         rhs_afm(ic), start=(ic == 0),
                                         stop=(ic == NIC - 1))
                        return r
                    P.op("tensor", g_h, reads=[bwb[0], bafm], writes=[bmm[5]],
                         tag="hm")
                    P.op("vector", lambda e: e.tensor_tensor(
                        scb[:, 0:512], mm[5][:, 0:QT], shf[:, 0:512], ALU.mult),
                         reads=[bmm[5], bshf], writes=[bscb], tag="hw")

                    def extra_moe(e, wi, bank):
                        return e.matmul(mm[bank][:, 0:QT], wb[wi][:, 4096:4224],
                                        scb[:, 0:512], start=False, stop=True)
                    emit_final_q(q, xin_d, bxin, xout_d, bxo, "w_l0base",
                                 4224, extra_moe, [bscb], 0, "f0")
                else:
                    emit_l1_route(q)

                    def extra_delta(e, wi, bank):
                        r = None
                        for cc in range(8):
                            r = e.matmul(mm[bank][:, 0:QT],
                                         wb[wi][:, 4096 + cc * 128:4096 + (cc + 1) * 128],
                                         scb[:, cc * 512:(cc + 1) * 512],
                                         start=False, stop=(cc == 7))
                        return r
                    emit_final_q(q, xin_d, bxin, xout_d, bxo, "w_l1base",
                                 5120, extra_delta, [bscb], 0, "f1")

        chain = [(x_in, bX), (xresA, bxA), (xresB, bxB), (xresA, bxA)]
        subs = [(0, "attn"), (0, "ffn"), (1, "attn"), (1, "ffn")]
        cur = 0
        for i, k in enumerate(sub_ids):
            layer, kindk = subs[k]
            xin_d, bxin = chain[cur]
            if i == len(sub_ids) - 1:
                xo_d, bxo = y_out, bY
            else:
                xo_d, bxo = chain[cur + 1]
            if kindk == "attn":
                sub_attn(layer, xin_d, bxin, xo_d, bxo)
            else:
                sub_ffn(layer, xin_d, bxin, xo_d, bxo)
            cur += 1

        with contextlib.ExitStack() as es2:
            block = es2.enter_context(nc.Block())
            P.materialize(block, trunc=trunc)

    build_program.last_P = P
    return nc


_consts = None


def stage_inputs(inputs):
    global _consts
    if _consts is None:
        _consts = make_consts()
    w = prep_weights(inputs)
    shared = {k: np.ascontiguousarray(v) for k, v in w.items()}
    for k, v in _consts.items():
        shared[k] = np.ascontiguousarray(v)
    x = np.ascontiguousarray(np.asarray(inputs["L0_attn_hidden_states"],
                                        dtype=np.float32))
    in_maps = []
    for c in range(8):
        m = dict(shared)
        m["x"] = np.ascontiguousarray(x[c])
        in_maps.append(m)
    return in_maps


def kernel(**inputs):
    in_maps = stage_inputs(inputs)
    nc = build_program()
    res = run_bass_kernel_spmd(nc, in_maps, core_ids=list(range(8)))
    return np.stack([np.asarray(r["y"], dtype=np.float32) for r in res.results])
